# version 12
# baseline (speedup 1.0000x reference)
import contextlib

import numpy as np
import concourse.bass as bass
import concourse.mybir as mybir
from concourse.bass_utils import run_bass_kernel_spmd

F32 = mybir.dt.float32
BF16 = mybir.dt.bfloat16
AF = mybir.ActivationFunctionType
AX = mybir.AxisListType

S = 2048
D = 1024
DH = 4096
NP = D // 128
NJH = DH // 128
PS = 2064
HALF = 1024
NEG = -1.0e9

ENGS = ("gpsimd", "sync", "tensor", "scalar", "vector")


class Build:

    def __init__(self, nc, sems):
        self.nc = nc
        self.sems = sems
        self.cnt = {k: 0 for k in sems}
        self.q = {e: [] for e in ENGS}
        self.ready = {}
        self.free = {}

    def add(self, eng, method, *args, waits=(), rd=(), fr=(), inc=None, **kw):
        ws = list(waits)
        for k in rd:
            if k in self.ready:
                ws.append(self.ready[k])
        for k in fr:
            if k in self.free:
                ws.append(self.free[k])
        item = {"m": method, "a": args, "k": kw, "w": ws, "i": []}
        self.q[eng].append(item)
        if inc is not None:
            amt = 16 if inc in ("DG", "DS") else 1
            self.cnt[inc] += amt
            item["i"].append((inc, amt))
            return self.cnt[inc]
        return None

    def mark_ready(self, key, sem):
        self.ready[key] = (sem, self.cnt[sem])

    def mark_free(self, key, sem):
        self.free[key] = (sem, self.cnt[sem])

    def barrier_counts(self):
        return dict(self.cnt)

    def barrier(self, snap):
        for e in ENGS:
            self.q[e].append({"m": None, "a": (), "k": {},
                              "w": [(s, v) for s, v in snap.items() if v > 0],
                              "i": []})

    def emit(self):
        return

    def emit_all(self):
        nc, sems = self.nc, self.sems
        if not hasattr(self, "_marks"):
            self._marks = {e: {} for e in ENGS}
        marks = self._marks

        def runner(eng_name):
            def run(eng):
                wm = marks[eng_name]
                for it in self.q[eng_name]:
                    for s, v in it["w"]:
                        if v > wm.get(s, 0):
                            eng.wait_ge(sems[s], v)
                            wm[s] = v
                    if it["m"] is None:
                        continue
                    inst = getattr(eng, it["m"])(*it["a"], **it["k"])
                    for s, amt in it["i"]:
                        inst.then_inc(sems[s], amt)
            return run

        with nc.Block() as block:
            for e in ENGS:
                if any(it["m"] is not None or it["w"] for it in self.q[e]):
                    getattr(block, e)(runner(e))
        self.q = {e: [] for e in ENGS}


def build_nc(debug=False, phases=99, p2_limit=4096):
    nc = bass.Bass("TRN2", target_bir_lowering=False)

    x_pk = nc.dram_tensor("x_pk", [128, NP * S], F32, kind="ExternalInput").ap()
    wq_pk = nc.dram_tensor("wq_pk", [128, 8192], F32, kind="ExternalInput").ap()
    wk_pk = nc.dram_tensor("wk_pk", [128, 8192], F32, kind="ExternalInput").ap()
    wvt_pk = nc.dram_tensor("wvt_pk", [128, 8192], F32, kind="ExternalInput").ap()
    wo_pk = nc.dram_tensor("wo_pk", [128, 8192], F32, kind="ExternalInput").ap()
    w1_pk = nc.dram_tensor("w1_pk", [128, 32768], F32, kind="ExternalInput").ap()
    w2_pk = nc.dram_tensor("w2_pk", [128, 32768], F32, kind="ExternalInput").ap()
    h1_pk = nc.dram_tensor("h1_pk", [128, 131072], F32, kind="ExternalInput").ap()
    h2_pk = nc.dram_tensor("h2_pk", [128, 131072], F32, kind="ExternalInput").ap()
    w3_pk = nc.dram_tensor("w3_pk", [128, 32768], F32, kind="ExternalInput").ap()
    s1_pk = nc.dram_tensor("s1_pk", [128, 32], F32, kind="ExternalInput").ap()
    s2_pk = nc.dram_tensor("s2_pk", [128, 32], F32, kind="ExternalInput").ap()
    sh1_pk = nc.dram_tensor("sh1_pk", [128, 32], F32, kind="ExternalInput").ap()
    sh2_pk = nc.dram_tensor("sh2_pk", [128, 32], F32, kind="ExternalInput").ap()
    s3_pk = nc.dram_tensor("s3_pk", [128, 8], F32, kind="ExternalInput").ap()
    maskp = nc.dram_tensor("maskp", [128, 672], F32, kind="ExternalInput").ap()
    ident_in = nc.dram_tensor("ident_in", [128, 128], F32, kind="ExternalInput").ap()
    ones_in = nc.dram_tensor("ones_in", [128, 128], F32, kind="ExternalInput").ap()
    out_pk = nc.dram_tensor("out_pk", [128, NP * S], F32, kind="ExternalOutput").ap()
    r_scr = nc.dram_tensor("r_scr", [128, NP * S], F32, kind="Internal").ap()
    dbg = {}
    if debug:
        for nm, cols in (("d_xn", NP * PS), ("d_q", NP * S), ("d_k", NP * PS),
                         ("d_v", 19 * 1024), ("d_op", NP * S), ("d_r", NP * S),
                         ("d_a", 32768), ("d_b", 32768), ("d_s0", 288),
                         ("d_e0", 288), ("d_nm", 1), ("d_nm2", 2), ("d_pb", 288), ("d_pt", 512)):
            dbg[nm] = nc.dram_tensor(nm, [128, cols], F32, kind="ExternalOutput").ap()

    sem_cm = [nc.semaphore(n) for n in ("DG", "DS", "MM", "SC", "VC")]
    sems = dict(zip(("DG", "DS", "MM", "SC", "VC"), [s.__enter__() for s in sem_cm]))
    E = Build(nc, sems)

    with contextlib.ExitStack() as st0:
        SB = lambda stk, name, shape, dt: stk.enter_context(nc.sbuf_tensor(name, shape, dt))
        PT_ = lambda stk, name, shape, dt: stk.enter_context(nc.psum_tensor(name, shape, dt))

        xn = SB(st0, "xn", [128, NP * PS], BF16)
        ident = SB(st0, "ident", [128, 128], BF16)
        ones = SB(st0, "ones", [128, 128], BF16)
        mk = SB(st0, "mk", [128, 672], BF16)
        s1s = SB(st0, "s1s", [128, 32], F32)
        s2s = SB(st0, "s2s", [128, 32], F32)
        sh1s = SB(st0, "sh1s", [128, 32], F32)
        sh2s = SB(st0, "sh2s", [128, 32], F32)
        s3s = SB(st0, "s3s", [128, 8], F32)
        epsb = SB(st0, "epsb", [128, 1], F32)
        neg16 = SB(st0, "neg16", [128, 1], F32)

        for dst, src in ((ident, ident_in), (ones, ones_in), (mk, maskp),
                         (s1s, s1_pk), (s2s, s2_pk), (sh1s, sh1_pk),
                         (sh2s, sh2_pk), (s3s, s3_pk)):
            cast = dst in (ident, ones, mk)
            E.add("gpsimd" if cast else "sync", "dma_start",
                  dst[:, :], src, inc="DG" if cast else "DS")
        E.mark_ready("consts", "DS")
        E.mark_ready("constsg", "DG")
        E.mark_ready("ident", "DG")
        E.add("vector", "memset", epsb[:, :], 1e-5, inc="VC")
        E.add("vector", "memset", neg16[:, :], -16.0, inc="VC")
        E.add("vector", "memset", xn[:, :], 0.0, inc="VC")

        with contextlib.ExitStack() as st1:
            xbf = SB(st1, "xbf", [128, NP * S], BF16)
            tmpA = (SB(st1, "tmpA0", [128, S], BF16), SB(st1, "tmpA1", [128, S], BF16))
            tmpB = SB(st1, "tmpB", [128, S], BF16)
            mean16 = SB(st1, "mean16", [128, S], BF16)
            rstd16 = SB(st1, "rstd16", [128, S], BF16)
            msqb = SB(st1, "msqb", [128, S], F32)
            varb = SB(st1, "varb", [128, S], F32)
            psA = PT_(st1, "psA", [128, S], F32)
            psB = PT_(st1, "psB", [128, S], F32)

            for k in range(NP):
                E.add("gpsimd", "dma_start", xbf[:, k * S:(k + 1) * S],
                      x_pk[:, k * S:(k + 1) * S], inc="DG")
                E.mark_ready(f"xb{k}", "DG")
            for k in range(NP):
                E.add("vector", "tensor_mul", tmpA[k % 2][:, :],
                      xbf[:, k * S:(k + 1) * S], xbf[:, k * S:(k + 1) * S],
                      rd=(f"xb{k}",), fr=(f"tmpA{k % 2}",), inc="VC")
                E.mark_ready(f"sqr{k}", "VC")
                for tc in range(4):
                    E.add("tensor", "matmul", psA[:, tc * 512:(tc + 1) * 512],
                          ones[:, :], xbf[:, k * S + tc * 512: k * S + (tc + 1) * 512],
                          start=(k == 0), stop=(k == NP - 1),
                          rd=(f"xb{k}", "constsg") if tc == 0 else (),
                          inc="MM" if (k == NP - 1 and tc == 3) else None)
                for tc in range(4):
                    E.add("tensor", "matmul", psB[:, tc * 512:(tc + 1) * 512],
                          ones[:, :], tmpA[k % 2][:, tc * 512:(tc + 1) * 512],
                          start=(k == 0), stop=(k == NP - 1),
                          rd=(f"sqr{k}",) if tc == 0 else (),
                          inc="MM" if (tc == 3) else None)
                E.mark_free(f"tmpA{k % 2}", "MM")
            E.mark_ready("psA", "MM")
            E.mark_ready("psB", "MM")
            E.add("scalar", "mul", mean16[:, :], psA[:, :], 1.0 / D, rd=("psA", "psB"), inc="SC")
            E.add("scalar", "square", msqb[:, :], mean16[:, :], inc="SC")
            E.mark_ready("msq", "SC")
            E.add("scalar", "mul", varb[:, :], psB[:, :], 1.0 / D, inc="SC")
            E.mark_ready("varE", "SC")
            E.add("vector", "tensor_sub", varb[:, :], varb[:, :], msqb[:, :],
                  rd=("varE",), inc="VC")
            E.mark_ready("var", "VC")
            E.add("scalar", "activation", msqb[:, :], varb[:, :], AF.Ln,
                  bias=epsb[:, :], rd=("var",))
            E.add("scalar", "activation", rstd16[:, :], msqb[:, :], AF.Exp,
                  scale=-0.5, inc="SC")
            E.mark_ready("st16", "SC")
            for k in range(NP):
                E.add("vector", "tensor_sub", tmpB[:, :], xbf[:, k * S:(k + 1) * S],
                      mean16[:, :], rd=("st16",) if k == 0 else ())
                E.add("vector", "tensor_mul", xn[:, k * PS + 8: k * PS + 8 + S],
                      tmpB[:, :], rstd16[:, :], inc="VC")
            E.mark_ready("xn", "VC")
            snap0 = E.barrier_counts()
            E.emit()

        if phases >= 2:
          with contextlib.ExitStack() as st2:
            o_pre = SB(st2, "o_pre", [128, NP * S], BF16)
            wX = SB(st2, "wX", [128, 8192], BF16)
            wY = SB(st2, "wY", [128, 8192], BF16)
            E.barrier(snap0)

            with contextlib.ExitStack() as st3:
                q_sb = SB(st3, "q_sb", [128, NP * S], BF16)
                k_pad = SB(st3, "k_pad", [128, NP * PS], BF16)
                v_sh = SB(st3, "v_sh", [128, 19 * 1024], BF16)
                Pm = (SB(st3, "Pm0", [128, 224], BF16), SB(st3, "Pm1", [128, 224], BF16),
                      SB(st3, "Pm2", [128, 224], BF16), SB(st3, "Pm3", [128, 224], BF16))
                rb = (SB(st3, "rb0", [128, 224], F32), SB(st3, "rb1", [128, 224], F32),
                      SB(st3, "rb2", [128, 224], F32))

                with contextlib.ExitStack() as st4:
                    psq = (PT_(st4, "psq0", [128, 512], F32), PT_(st4, "psq1", [128, 512], F32))
                    psv = (PT_(st4, "psv0", [128, 1024], F32), PT_(st4, "psv1", [128, 1024], F32))
                    for jw in range(NP):
                        E.add("gpsimd", "dma_start",
                              wX[:, jw * 1024:(jw + 1) * 1024],
                              wq_pk[:, jw * 1024:(jw + 1) * 1024], inc="DG")
                        E.mark_ready(f"wq{jw}", "DG")
                    E.mark_ready("wq", "DG")
                    E.add("gpsimd", "dma_start", wY[:, :], wk_pk, inc="DG")
                    E.mark_ready("wk", "DG")
                    for k in range(NP):
                        E.add("vector", "memset", k_pad[:, k * PS: k * PS + 8], 0.0,
                              inc="VC" if k == NP - 1 else None)
                        E.add("vector", "memset", k_pad[:, k * PS + 8 + S: (k + 1) * PS], 0.0,
                              inc="VC" if k == NP - 1 else None)
                    E.mark_ready("kpad0", "VC")

                    def proj(w_sb, wkey, outfn, scale, perj=False):
                        for j in range(NP):
                            for tc in range(4):
                                idx = j * 4 + tc
                                for k in range(NP):
                                    rdk = ()
                                    if k == 0 and tc == 0:
                                        rdk = (f"{wkey}{j}",) if perj else ((wkey,) if j == 0 else ())
                                    E.add("tensor", "matmul",
                                          psq[idx % 2][:, :],
                                          w_sb[:, (j * NP + k) * 128:(j * NP + k + 1) * 128],
                                          xn[:, k * PS + 8 + tc * 512: k * PS + 8 + (tc + 1) * 512],
                                          start=(k == 0), stop=(k == NP - 1),
                                          rd=rdk,
                                          fr=(f"psq{idx % 2}",) if k == 0 else (),
                                          inc="MM" if k == NP - 1 else None)
                                E.mark_ready(f"pj{idx % 2}", "MM")
                                dst, rdx = outfn(j, tc)
                                if scale is None:
                                    E.add("scalar", "copy", dst, psq[idx % 2][:, :],
                                          rd=(f"pj{idx % 2}",) + rdx, inc="SC")
                                else:
                                    E.add("scalar", "activation", dst, psq[idx % 2][:, :],
                                          AF.Copy, scale=scale,
                                          rd=(f"pj{idx % 2}",) + rdx, inc="SC")
                                E.mark_free(f"psq{idx % 2}", "SC")

                    proj(wX, "wq",
                         lambda j, tc: (q_sb[:, j * S + tc * 512: j * S + (tc + 1) * 512], ()),
                         0.125, perj=True)
                    E.mark_ready("q", "SC")
                    E.mark_free("wX", "MM")
                    E.add("gpsimd", "dma_start", wX[:, :], wvt_pk, fr=("wX",), inc="DG")
                    E.mark_ready("wvt", "DG")
                    proj(wY, "wk",
                         lambda j, tc: (k_pad[:, j * PS + 8 + tc * 512: j * PS + 8 + (tc + 1) * 512],
                                        ("kpad0",) if (j == 0 and tc == 0) else ()),
                         None)
                    E.mark_ready("k", "SC")
                    E.mark_free("wY", "MM")
                    E.add("gpsimd", "dma_start", wY[:, :], wo_pk, fr=("wY",), inc="DG")
                    E.mark_ready("wo", "DG")

                    for c in range(19):
                        M = 128 if c < 18 else 48
                        for k in range(NP):
                            for oh in range(2):
                                E.add("tensor", "matmul",
                                      psv[c % 2][0:M, oh * 512:(oh + 1) * 512],
                                      xn[:, k * PS + c * 112: k * PS + c * 112 + M],
                                      wX[:, k * 1024 + oh * 512: k * 1024 + (oh + 1) * 512],
                                      start=(k == 0), stop=(k == NP - 1),
                                      rd=("wvt",) if (c == 0 and k == 0 and oh == 0) else (),
                                      fr=(f"psv{c % 2}",) if (k == 0 and oh == 0) else (),
                                      inc="MM" if (k == NP - 1 and oh == 1) else None)
                        E.mark_ready(f"pv{c % 2}", "MM")
                        E.add("scalar", "copy", v_sh[0:M, c * 1024:(c + 1) * 1024],
                              psv[c % 2][0:M, :], rd=(f"pv{c % 2}",), inc="SC")
                        E.mark_free(f"psv{c % 2}", "SC")
                    E.mark_ready("v", "SC")
                    E.emit()

                if phases >= 3:
                  with contextlib.ExitStack() as st5:
                      psm = (PT_(st5, "psm0", [128, 1024], F32), PT_(st5, "psm1", [128, 1024], F32))
                      psO = (PT_(st5, "psO0", [128, 1024], F32), PT_(st5, "psO1", [128, 1024], F32))
                      NC2 = 19
                      NIT = NP * NC2

                      def it_info(i):
                          jp, c = divmod(i, NC2)
                          tc = 112 if c < 18 else 32
                          kk = 128 if c < 18 else 48
                          moff = 224 if c == 0 else (448 if c == 18 else 0)
                          return jp, c, i % 2, tc, kk, moff

                      def T0(i):
                          jp, c, sl, tc, kk, moff = it_info(i)
                          for hh in range(2):
                              E.add("tensor", "matmul",
                                    psm[sl][0:kk, hh * 512: hh * 512 + tc],
                                    k_pad[hh * 64:(hh + 1) * 64, jp * PS + c * 112: jp * PS + c * 112 + kk],
                                    q_sb[hh * 64:(hh + 1) * 64, jp * S + c * 112: jp * S + c * 112 + tc],
                                    start=True, stop=True,
                                    rd=("q", "k", "v") if i == 0 and hh == 0 else (),
                                    fr=(f"psm{sl}",) if hh == 0 else (),
                                    inc="MM" if hh == 1 else None)
                          E.mark_ready(f"sS{i}", "MM")

                      def SC1(i):
                          jp, c, sl, tc, kk, moff = it_info(i)
                          p3 = i % 4
                          E.add("scalar", "activation",
                                bass.AP(Pm[p3], 0, [[224, kk], [112, 2], [1, tc]]),
                                bass.AP(psm[sl], 0, [[1024, kk], [512, 2], [1, tc]]),
                                AF.Exp, bias=neg16[0:kk, :],
                                rd=(f"sS{i}",), fr=(f"Pmf{p3}",), inc="SC")
                          E.mark_ready(f"ex{i}", "SC")
                          E.mark_free(f"psm{sl}", "SC")

                      def V1(i):
                          jp, c, sl, tc, kk, moff = it_info(i)
                          p3 = i % 4
                          E.add("vector", "tensor_mul", Pm[p3][0:kk, 0:224],
                                Pm[p3][0:kk, 0:224], mk[0:kk, moff:moff + 224],
                                rd=(f"ex{i}",), inc="VC")
                          E.mark_ready(f"pm{i}", "VC")

                      def T1(i):
                          jp, c, sl, tc, kk, moff = it_info(i)
                          p3 = i % 4
                          for hh in range(2):
                              E.add("tensor", "matmul",
                                    psO[sl][:, hh * 512: hh * 512 + tc],
                                    ones[0:kk, :],
                                    Pm[p3][0:kk, hh * 112: hh * 112 + tc],
                                    start=True, stop=True,
                                    rd=(f"pm{i}",) if hh == 0 else (),
                                    fr=(f"psO{sl}",) if hh == 0 else (),
                                    inc="MM" if hh == 1 else None)
                          E.mark_ready(f"sm{i}", "MM")

                      def SC2(i):
                          jp, c, sl, tc, kk, moff = it_info(i)
                          E.add("scalar", "activation",
                                bass.AP(rb[i % 3], 0, [[224, 128], [112, 2], [1, tc]]),
                                bass.AP(psO[sl], 0, [[1024, 128], [512, 2], [1, tc]]),
                                AF.Ln, rd=(f"sm{i}",), fr=(f"rbf{i % 3}",))
                          E.add("scalar", "activation",
                                bass.AP(rb[i % 3], 0, [[224, 128], [112, 2], [1, tc]]),
                                bass.AP(rb[i % 3], 0, [[224, 128], [112, 2], [1, tc]]),
                                AF.Exp, scale=-1.0, inc="SC")
                          E.mark_ready(f"rb{i}", "SC")

                      def T2(i):
                          jp, c, sl, tc, kk, moff = it_info(i)
                          p3 = i % 4
                          for hh in range(2):
                              tp = (0, 64 * hh)
                              E.add("tensor", "matmul",
                                    psO[sl][hh * 64:(hh + 1) * 64, hh * 512: hh * 512 + tc],
                                    v_sh[0:kk, c * 1024 + jp * 128 + hh * 64: c * 1024 + jp * 128 + (hh + 1) * 64],
                                    Pm[p3][0:kk, hh * 112: hh * 112 + tc],
                                    start=True, stop=True, tile_position=tp,
                                    rd=(f"rb{i}",) if hh == 0 else (),
                                    inc="MM" if hh == 1 else None)
                          E.mark_ready(f"av{i}", "MM")
                          E.mark_free(f"Pmf{p3}", "MM")

                      def V2(i):
                          jp, c, sl, tc, kk, moff = it_info(i)
                          for hh in range(2):
                              r0 = hh * 64
                              E.add("vector", "tensor_mul",
                                    o_pre[r0:r0 + 64, jp * S + c * 112: jp * S + c * 112 + tc],
                                    psO[sl][r0:r0 + 64, hh * 512: hh * 512 + tc],
                                    rb[i % 3][r0:r0 + 64, hh * 112: hh * 112 + tc],
                                    rd=(f"av{i}",) if hh == 0 else (),
                                    inc="VC" if hh == 1 else None)
                          E.mark_free(f"psO{sl}", "VC")
                          E.mark_free(f"rbf{i % 3}", "VC")

                      for k in range(NIT + 4):
                          if k < NIT:
                              T0(k)
                          if 0 <= k - 2 < NIT:
                              T1(k - 2)
                          if 0 <= k - 3 < NIT:
                              T2(k - 3)
                          if 0 <= k - 1 < NIT:
                              SC1(k - 1)
                          if 0 <= k - 2 < NIT:
                              SC2(k - 2)
                          if 0 <= k - 1 < NIT:
                              V1(k - 1)
                          if 0 <= k - 3 < NIT:
                              V2(k - 3)
                      E.mark_ready("o_pre", "VC")
                      if debug:
                          E.add("gpsimd", "dma_start", dbg["d_xn"], xn[:, :],
                                rd=("o_pre",), inc="DG")
                          E.add("gpsimd", "dma_start", dbg["d_q"], q_sb[:, :], inc="DG")
                          E.add("gpsimd", "dma_start", dbg["d_k"], k_pad[:, :], inc="DG")
                          E.add("gpsimd", "dma_start", dbg["d_v"], v_sh[:, :], inc="DG")
                          E.add("gpsimd", "dma_start", dbg["d_op"], o_pre[:, :], inc="DG")
                      snap2 = E.barrier_counts()
                      E.emit()

            if phases >= 4:
              with contextlib.ExitStack() as st6:
                  rst = (SB(st6, "rst0", [128, S], F32), SB(st6, "rst1", [128, S], F32))
                  xrs = (SB(st6, "xrs0", [128, S], F32), SB(st6, "xrs1", [128, S], F32))
                  pswo = (PT_(st6, "pswo0", [128, 512], F32), PT_(st6, "pswo1", [128, 512], F32))
                  E.barrier(snap2)
                  E.add("sync", "dma_start", xrs[0][:, :], x_pk[:, 0:S], inc="DS")
                  E.mark_ready("xj0", "DS")
                  for j in range(NP):
                      for tc in range(4):
                          idx = j * 4 + tc
                          for k in range(NP):
                              E.add("tensor", "matmul", pswo[idx % 2][:, :],
                                    wY[:, (j * NP + k) * 128:(j * NP + k + 1) * 128],
                                    o_pre[:, k * S + tc * 512: k * S + (tc + 1) * 512],
                                    start=(k == 0), stop=(k == NP - 1),
                                    rd=("wo", "o_pre") if idx == 0 and k == 0 else (),
                                    fr=(f"pswo{idx % 2}",) if k == 0 else (),
                                    inc="MM" if k == NP - 1 else None)
                          E.mark_ready(f"po{idx % 2}", "MM")
                          E.add("scalar", "copy",
                                rst[j % 2][:, tc * 512:(tc + 1) * 512],
                                pswo[idx % 2][:, :],
                                rd=(f"po{idx % 2}",), fr=(f"rstf{j % 2}",) if tc == 0 else (),
                                inc="SC")
                          E.mark_ready(f"rc{idx}", "SC")
                          E.mark_free(f"pswo{idx % 2}", "SC")
                          E.add("vector", "tensor_add",
                                rst[j % 2][:, tc * 512:(tc + 1) * 512],
                                rst[j % 2][:, tc * 512:(tc + 1) * 512],
                                xrs[j % 2][:, tc * 512:(tc + 1) * 512],
                                rd=(f"rc{idx}", f"xj{j}"), inc="VC")
                      E.mark_ready(f"rj{j}", "VC")
                      E.mark_free(f"xjf{j % 2}", "VC")
                      if j + 1 < NP:
                          E.add("sync", "dma_start", xrs[(j + 1) % 2][:, :],
                                x_pk[:, (j + 1) * S:(j + 2) * S],
                                fr=(f"xjf{(j + 1) % 2}",), inc="DS")
                          E.mark_ready(f"xj{j + 1}", "DS")
                      E.add("sync", "dma_start", r_scr[:, j * S:(j + 1) * S],
                            rst[j % 2][:, :], rd=(f"rj{j}",), inc="DS")
                      E.mark_free(f"rstf{j % 2}", "DS")
                  if debug:
                      E.add("gpsimd", "dma_start", dbg["d_r"], r_scr,
                            waits=[("DS", E.cnt["DS"])], inc="DG")
                  snap3 = E.barrier_counts()
                  E.emit()

        if phases >= 5:
          with contextlib.ExitStack() as st7:
            a_sb = SB(st7, "a_sb", [128, 32 * HALF], BF16)
            b_sb = SB(st7, "b_sb", [128, 32 * HALF], BF16)
            w1r = (SB(st7, "w1r0", [128, 1024], BF16), SB(st7, "w1r1", [128, 1024], BF16))
            w2r = (SB(st7, "w2r0", [128, 1024], BF16), SB(st7, "w2r1", [128, 1024], BF16))
            hr = (SB(st7, "hr0", [128, 4096], BF16), SB(st7, "hr1", [128, 4096], BF16))
            ta = (SB(st7, "ta0", [128, 512], F32), SB(st7, "ta1", [128, 512], F32))
            tb = (SB(st7, "tb0", [128, 512], F32), SB(st7, "tb1", [128, 512], F32))
            xre = (SB(st7, "xre0", [128, 512], F32), SB(st7, "xre1", [128, 512], F32))
            E.barrier(snap3)
            for H in range(2):
                toff = H * HALF
                with contextlib.ExitStack() as st8:
                    pg1 = (PT_(st8, f"pg10_{H}", [128, 1024], F32),
                           PT_(st8, f"pg11_{H}", [128, 1024], F32))
                    pg2 = (PT_(st8, f"pg20_{H}", [128, 1024], F32),
                           PT_(st8, f"pg21_{H}", [128, 1024], F32))
                    sc_stage = E.cnt["SC"]
                    for j in range(NJH):
                        E.add("gpsimd", "dma_start", w1r[j % 2][:, :],
                              w1_pk[:, j * 1024:(j + 1) * 1024],
                              fr=(f"w1r{j % 2}",), inc="DG")
                        E.add("gpsimd", "dma_start", w2r[j % 2][:, :],
                              w2_pk[:, j * 1024:(j + 1) * 1024],
                              fr=(f"w2r{j % 2}",), inc="DG")
                        E.mark_ready(f"w12_{j % 2}", "DG")
                        for k in range(NP):
                            for tc in range(2):
                                E.add("tensor", "matmul",
                                      pg1[j % 2][:, tc * 512:(tc + 1) * 512],
                                      w1r[j % 2][:, k * 128:(k + 1) * 128],
                                      xn[:, k * PS + 8 + toff + tc * 512: k * PS + 8 + toff + (tc + 1) * 512],
                                      start=(k == 0), stop=(k == NP - 1),
                                      waits=[("SC", sc_stage)] if (j == 0 and k == 0 and tc == 0) else (),
                                      rd=(f"w12_{j % 2}",) if k == 0 and tc == 0 else (),
                                      fr=(f"pg1{j % 2}", f"pg2{j % 2}") if k == 0 and tc == 0 else ())
                        for k in range(NP):
                            for tc in range(2):
                                E.add("tensor", "matmul",
                                      pg2[j % 2][:, tc * 512:(tc + 1) * 512],
                                      w2r[j % 2][:, k * 128:(k + 1) * 128],
                                      xn[:, k * PS + 8 + toff + tc * 512: k * PS + 8 + toff + (tc + 1) * 512],
                                      start=(k == 0), stop=(k == NP - 1),
                                      inc="MM" if (k == NP - 1 and tc == 1) else None)
                        E.mark_ready(f"pga{j % 2}", "MM")
                        E.mark_free(f"w1r{j % 2}", "MM")
                        E.mark_free(f"w2r{j % 2}", "MM")
                        for tc in range(2):
                            i2 = (2 * j + tc) % 2
                            E.add("scalar", "activation", ta[i2][:, :],
                                  pg1[j % 2][:, tc * 512:(tc + 1) * 512], AF.Silu,
                                  scale=s1s[:, j:j + 1],
                                  rd=(f"pga{j % 2}",), fr=(f"taf{i2}",))
                            E.add("scalar", "activation", tb[i2][:, :],
                                  pg2[j % 2][:, tc * 512:(tc + 1) * 512], AF.Copy,
                                  scale=s2s[:, j:j + 1], inc="SC")
                            E.mark_ready(f"tl{i2}", "SC")
                            if tc == 1:
                                E.mark_free(f"pg1{j % 2}", "SC")
                                E.mark_free(f"pg2{j % 2}", "SC")
                            E.add("vector", "tensor_mul",
                                  a_sb[:, j * 1024 + tc * 512: j * 1024 + (tc + 1) * 512],
                                  ta[i2][:, :], tb[i2][:, :], rd=(f"tl{i2}",), inc="VC")
                            E.mark_free(f"taf{i2}", "VC")
                    E.mark_ready("a", "VC")
                    E.emit()

                for stage, (hpk, ssb, src, dst) in enumerate((
                        (h1_pk, sh1s, a_sb, b_sb),
                        (h2_pk, sh2s, b_sb, a_sb))):
                    with contextlib.ExitStack() as st9:
                        psb = (PT_(st9, f"psb0_{H}_{stage}", [128, 1024], F32),
                               PT_(st9, f"psb1_{H}_{stage}", [128, 1024], F32))
                        for j in range(NJH):
                            E.add("gpsimd", "dma_start", hr[j % 2][:, :],
                                  hpk[:, j * 4096:(j + 1) * 4096],
                                  fr=(f"hr{j % 2}",), inc="DG")
                            E.mark_ready(f"hl{j % 2}", "DG")
                            for k in range(NJH):
                                for tc in range(2):
                                    E.add("tensor", "matmul",
                                          psb[j % 2][:, tc * 512:(tc + 1) * 512],
                                          hr[j % 2][:, k * 128:(k + 1) * 128],
                                          src[:, k * 1024 + tc * 512: k * 1024 + (tc + 1) * 512],
                                          start=(k == 0), stop=(k == NJH - 1),
                                          rd=((f"hl{j % 2}",) + (("a",) if stage == 0 else ("c_in",))
                                              ) if k == 0 and tc == 0 else (),
                                          fr=(f"psb{j % 2}",) if k == 0 and tc == 0 else (),
                                          inc="MM" if (k == NJH - 1 and tc == 1) else None)
                            E.mark_ready(f"pb{j % 2}", "MM")
                            E.mark_free(f"hr{j % 2}", "MM")
                            for tc in range(2):
                                E.add("scalar", "activation",
                                      dst[:, j * 1024 + tc * 512: j * 1024 + (tc + 1) * 512],
                                      psb[j % 2][:, tc * 512:(tc + 1) * 512], AF.Gelu,
                                      scale=ssb[:, j:j + 1], rd=(f"pb{j % 2}",),
                                      inc="SC" if tc == 1 else None)
                            E.mark_free(f"psb{j % 2}", "SC")
                        E.mark_ready("c_in" if stage == 0 else "cc", "SC")
                        E.emit()

                with contextlib.ExitStack() as st10:
                    psf = (PT_(st10, f"psf0_{H}", [128, 1024], F32),
                           PT_(st10, f"psf1_{H}", [128, 1024], F32))
                    for j in range(NP):
                        E.add("gpsimd", "dma_start", hr[j % 2][:, :],
                              w3_pk[:, j * 4096:(j + 1) * 4096],
                              fr=(f"hr{j % 2}",), inc="DG")
                        E.mark_ready(f"w3l{j % 2}", "DG")
                        for k in range(NJH):
                            for tc in range(2):
                                E.add("tensor", "matmul",
                                      psf[j % 2][:, tc * 512:(tc + 1) * 512],
                                      hr[j % 2][:, k * 128:(k + 1) * 128],
                                      a_sb[:, k * 1024 + tc * 512: k * 1024 + (tc + 1) * 512],
                                      start=(k == 0), stop=(k == NJH - 1),
                                      rd=(f"w3l{j % 2}", "cc") if k == 0 and tc == 0 else (),
                                      fr=(f"psf{j % 2}",) if k == 0 and tc == 0 else (),
                                      inc="MM" if (k == NJH - 1 and tc == 1) else None)
                        E.mark_ready(f"pf{j % 2}", "MM")
                        E.mark_free(f"hr{j % 2}", "MM")
                        for tc in range(2):
                            idx = j * 2 + tc
                            i2 = idx % 2
                            coff = j * S + toff + tc * 512
                            E.add("sync", "dma_start", xre[i2][:, :],
                                  r_scr[:, coff: coff + 512], fr=(f"xref{i2}",), inc="DS")
                            E.mark_ready(f"xr{idx}", "DS")
                            E.add("scalar", "activation", ta[i2][:, :],
                                  psf[j % 2][:, tc * 512:(tc + 1) * 512], AF.Copy,
                                  scale=s3s[:, j:j + 1],
                                  rd=(f"pf{j % 2}",), fr=(f"taf{i2}",),
                                  inc="SC")
                            E.mark_ready(f"ft{idx}", "SC")
                            if tc == 1:
                                E.mark_free(f"psf{j % 2}", "SC")
                            E.add("vector", "tensor_add", tb[i2][:, :],
                                  ta[i2][:, :], xre[i2][:, :],
                                  rd=(f"ft{idx}", f"xr{idx}"), fr=(f"tbf{i2}",), inc="VC")
                            E.mark_ready(f"ot{idx}", "VC")
                            E.mark_free(f"taf{i2}", "VC")
                            E.mark_free(f"xref{i2}", "VC")
                            E.add("sync", "dma_start",
                                  out_pk[:, coff: coff + 512], tb[i2][:, :],
                                  rd=(f"ot{idx}",), inc="DS")
                            E.mark_free(f"tbf{i2}", "DS")
                    if H == 1 and debug:
                        E.add("gpsimd", "dma_start", dbg["d_a"], a_sb[:, :], inc="DG")
                        E.add("gpsimd", "dma_start", dbg["d_b"], b_sb[:, :], inc="DG")
                    E.emit()
            E.add("gpsimd", "nop", waits=[("DG", E.cnt["DG"]), ("DS", E.cnt["DS"])])
            E.add("sync", "nop", waits=[("DG", E.cnt["DG"]), ("DS", E.cnt["DS"])])
            E.emit()

    E.emit_all()
    for s in sem_cm:
        s.__exit__(None, None, None)
    return nc


def _pack_lhsT(W):
    O, I = W.shape
    A = W.reshape(O // 128, 128, I // 128, 128)
    return np.ascontiguousarray(A.transpose(3, 0, 2, 1).reshape(128, -1))


def _pack_fm(X):
    T, C = X.shape
    return np.ascontiguousarray(X.T.reshape(C // 128, 128, T).transpose(1, 0, 2).reshape(128, -1))


def _masks():
    j = np.arange(128)[:, None]
    tp_ = np.arange(112)[None, :]
    band = (j >= tp_) & (j <= tp_ + 15)
    mid = band
    first = band & (j >= 8)
    last = band & (tp_ < 32) & (j < 40)
    pair = lambda m: np.concatenate([m, m], axis=1).astype(np.float32)
    return np.ascontiguousarray(np.concatenate([pair(mid), pair(first), pair(last)], axis=1))


def kernel(L0_attn_x, L0_attn_Wq, L0_attn_Wk, L0_attn_Wv, L0_attn_Wo,
           L0_ffn_w1_w, L0_ffn_w1_s, L0_ffn_w2_w, L0_ffn_w2_s,
           L0_ffn_h1_w, L0_ffn_h1_s, L0_ffn_h2_w, L0_ffn_h2_s,
           L0_ffn_w3_w, L0_ffn_w3_s):
    f32 = lambda a: np.asarray(a, dtype=np.float32)
    x = f32(L0_attn_x)
    shared = {
        "wq_pk": _pack_lhsT(f32(L0_attn_Wq)),
        "wk_pk": _pack_lhsT(f32(L0_attn_Wk)),
        "wvt_pk": np.ascontiguousarray(
            f32(L0_attn_Wv).T.reshape(8, 128, 1024).transpose(1, 0, 2).reshape(128, 8192)),
        "wo_pk": _pack_lhsT(f32(L0_attn_Wo)),
        "w1_pk": _pack_lhsT(f32(L0_ffn_w1_w)),
        "w2_pk": _pack_lhsT(f32(L0_ffn_w2_w)),
        "h1_pk": _pack_lhsT(f32(L0_ffn_h1_w)),
        "h2_pk": _pack_lhsT(f32(L0_ffn_h2_w)),
        "w3_pk": _pack_lhsT(f32(L0_ffn_w3_w)),
        "s1_pk": np.ascontiguousarray(f32(L0_ffn_w1_s).reshape(32, 128).T),
        "s2_pk": np.ascontiguousarray(f32(L0_ffn_w2_s).reshape(32, 128).T),
        "sh1_pk": np.ascontiguousarray(f32(L0_ffn_h1_s).reshape(32, 128).T),
        "sh2_pk": np.ascontiguousarray(f32(L0_ffn_h2_s).reshape(32, 128).T),
        "s3_pk": np.ascontiguousarray(f32(L0_ffn_w3_s).reshape(8, 128).T),
        "maskp": _masks(),
        "ident_in": np.eye(128, dtype=np.float32),
        "ones_in": np.ones((128, 128), dtype=np.float32),
    }
    nc = build_nc()
    in_maps = [dict(shared, x_pk=_pack_fm(x[b])) for b in range(8)]
    res = run_bass_kernel_spmd(nc, in_maps, core_ids=list(range(8)))
    outs = []
    for r in res.results:
        o = np.asarray(r["out_pk"])
        outs.append(o.reshape(128, NP, S).transpose(2, 1, 0).reshape(S, D))
    return np.ascontiguousarray(np.stack(outs)).astype(np.float32)


# version 15
# speedup vs baseline: 1.0006x; 1.0006x over previous
import contextlib

import numpy as np
import concourse.bass as bass
import concourse.mybir as mybir
from concourse.bass_utils import run_bass_kernel_spmd

F32 = mybir.dt.float32
BF16 = mybir.dt.bfloat16
AF = mybir.ActivationFunctionType
AX = mybir.AxisListType

S = 2048
D = 1024
DH = 4096
NP = D // 128
NJH = DH // 128
PS = 2064
HALF = 1024
NEG = -1.0e9

ENGS = ("gpsimd", "sync", "tensor", "scalar", "vector")


class Build:

    def __init__(self, nc, sems):
        self.nc = nc
        self.sems = sems
        self.cnt = {k: 0 for k in sems}
        self.q = {e: [] for e in ENGS}
        self.ready = {}
        self.free = {}

    def add(self, eng, method, *args, waits=(), rd=(), fr=(), inc=None, **kw):
        ws = list(waits)
        for k in rd:
            if k in self.ready:
                ws.append(self.ready[k])
        for k in fr:
            if k in self.free:
                ws.append(self.free[k])
        item = {"m": method, "a": args, "k": kw, "w": ws, "i": []}
        self.q[eng].append(item)
        if inc is not None:
            amt = 16 if inc in ("DG", "DS") else 1
            self.cnt[inc] += amt
            item["i"].append((inc, amt))
            return self.cnt[inc]
        return None

    def mark_ready(self, key, sem):
        self.ready[key] = (sem, self.cnt[sem])

    def mark_free(self, key, sem):
        self.free[key] = (sem, self.cnt[sem])

    def barrier_counts(self):
        return dict(self.cnt)

    def barrier(self, snap):
        for e in ENGS:
            self.q[e].append({"m": None, "a": (), "k": {},
                              "w": [(s, v) for s, v in snap.items() if v > 0],
                              "i": []})

    def emit(self):
        return

    def emit_all(self):
        nc, sems = self.nc, self.sems
        if not hasattr(self, "_marks"):
            self._marks = {e: {} for e in ENGS}
        marks = self._marks

        def runner(eng_name):
            def run(eng):
                wm = marks[eng_name]
                for it in self.q[eng_name]:
                    for s, v in it["w"]:
                        if v > wm.get(s, 0):
                            eng.wait_ge(sems[s], v)
                            wm[s] = v
                    if it["m"] is None:
                        continue
                    inst = getattr(eng, it["m"])(*it["a"], **it["k"])
                    for s, amt in it["i"]:
                        inst.then_inc(sems[s], amt)
            return run

        with nc.Block() as block:
            for e in ENGS:
                if any(it["m"] is not None or it["w"] for it in self.q[e]):
                    getattr(block, e)(runner(e))
        self.q = {e: [] for e in ENGS}


def build_nc(debug=False, phases=99, p2_limit=4096):
    nc = bass.Bass("TRN2", target_bir_lowering=False)

    x_pk = nc.dram_tensor("x_pk", [128, NP * S], F32, kind="ExternalInput").ap()
    wq_pk = nc.dram_tensor("wq_pk", [128, 8192], F32, kind="ExternalInput").ap()
    wk_pk = nc.dram_tensor("wk_pk", [128, 8192], F32, kind="ExternalInput").ap()
    wvt_pk = nc.dram_tensor("wvt_pk", [128, 8192], F32, kind="ExternalInput").ap()
    wo_pk = nc.dram_tensor("wo_pk", [128, 8192], F32, kind="ExternalInput").ap()
    w1_pk = nc.dram_tensor("w1_pk", [128, 32768], F32, kind="ExternalInput").ap()
    w2_pk = nc.dram_tensor("w2_pk", [128, 32768], F32, kind="ExternalInput").ap()
    h1_pk = nc.dram_tensor("h1_pk", [128, 131072], F32, kind="ExternalInput").ap()
    h2_pk = nc.dram_tensor("h2_pk", [128, 131072], F32, kind="ExternalInput").ap()
    w3_pk = nc.dram_tensor("w3_pk", [128, 32768], F32, kind="ExternalInput").ap()
    s1_pk = nc.dram_tensor("s1_pk", [128, 32], F32, kind="ExternalInput").ap()
    s2_pk = nc.dram_tensor("s2_pk", [128, 32], F32, kind="ExternalInput").ap()
    sh1_pk = nc.dram_tensor("sh1_pk", [128, 32], F32, kind="ExternalInput").ap()
    sh2_pk = nc.dram_tensor("sh2_pk", [128, 32], F32, kind="ExternalInput").ap()
    s3_pk = nc.dram_tensor("s3_pk", [128, 8], F32, kind="ExternalInput").ap()
    maskp = nc.dram_tensor("maskp", [128, 672], F32, kind="ExternalInput").ap()
    ones_in = nc.dram_tensor("ones_in", [128, 128], F32, kind="ExternalInput").ap()
    out_pk = nc.dram_tensor("out_pk", [128, NP * S], F32, kind="ExternalOutput").ap()
    r_scr = nc.dram_tensor("r_scr", [128, NP * S], F32, kind="Internal").ap()
    dbg = {}
    if debug:
        for nm, cols in (("d_xn", NP * PS), ("d_q", NP * S), ("d_k", NP * PS),
                         ("d_v", 19 * 1024), ("d_op", NP * S), ("d_r", NP * S),
                         ("d_a", 32768), ("d_b", 32768), ("d_s0", 288),
                         ("d_e0", 288), ("d_nm", 1), ("d_nm2", 2), ("d_pb", 288), ("d_pt", 512)):
            dbg[nm] = nc.dram_tensor(nm, [128, cols], F32, kind="ExternalOutput").ap()

    sem_cm = [nc.semaphore(n) for n in ("DG", "DS", "MM", "SC", "VC")]
    sems = dict(zip(("DG", "DS", "MM", "SC", "VC"), [s.__enter__() for s in sem_cm]))
    E = Build(nc, sems)

    with contextlib.ExitStack() as st0:
        SB = lambda stk, name, shape, dt: stk.enter_context(nc.sbuf_tensor(name, shape, dt))
        PT_ = lambda stk, name, shape, dt: stk.enter_context(nc.psum_tensor(name, shape, dt))

        xn = SB(st0, "xn", [128, NP * PS], BF16)
        ones = SB(st0, "ones", [128, 128], BF16)
        mk = SB(st0, "mk", [128, 672], BF16)
        s1s = SB(st0, "s1s", [128, 32], F32)
        s2s = SB(st0, "s2s", [128, 32], F32)
        sh1s = SB(st0, "sh1s", [128, 32], F32)
        sh2s = SB(st0, "sh2s", [128, 32], F32)
        s3s = SB(st0, "s3s", [128, 8], F32)
        epsb = SB(st0, "epsb", [128, 1], F32)
        neg16 = SB(st0, "neg16", [128, 1], F32)

        for dst, src in ((ones, ones_in), (mk, maskp),
                         (s1s, s1_pk), (s2s, s2_pk), (sh1s, sh1_pk),
                         (sh2s, sh2_pk), (s3s, s3_pk)):
            cast = dst in (ones, mk)
            E.add("gpsimd" if cast else "sync", "dma_start",
                  dst[:, :], src, inc="DG" if cast else "DS")
            if dst is ones:
                E.mark_ready("constsg", "DG")
        E.mark_ready("consts", "DS")
        E.mark_ready("ident", "DG")
        E.add("vector", "memset", epsb[:, :], 1e-5, inc="VC")
        E.add("vector", "memset", neg16[:, :], -16.0, inc="VC")
        E.add("vector", "memset", xn[:, :], 0.0, inc="VC")

        with contextlib.ExitStack() as st1:
            xbf = SB(st1, "xbf", [128, NP * S], BF16)
            tmpA = (SB(st1, "tmpA0", [128, S], BF16), SB(st1, "tmpA1", [128, S], BF16))
            tmpB = SB(st1, "tmpB", [128, S], BF16)
            mean16 = SB(st1, "mean16", [128, S], BF16)
            rstd16 = SB(st1, "rstd16", [128, S], BF16)
            msqb = SB(st1, "msqb", [128, S], F32)
            varb = SB(st1, "varb", [128, S], F32)
            psA = PT_(st1, "psA", [128, S], F32)
            psB = PT_(st1, "psB", [128, S], F32)

            for k in range(NP):
                E.add("gpsimd", "dma_start", xbf[:, k * S:(k + 1) * S],
                      x_pk[:, k * S:(k + 1) * S], inc="DG")
                E.mark_ready(f"xb{k}", "DG")
            for k in range(NP):
                E.add("vector", "tensor_mul", tmpA[k % 2][:, :],
                      xbf[:, k * S:(k + 1) * S], xbf[:, k * S:(k + 1) * S],
                      rd=(f"xb{k}",), fr=(f"tmpA{k % 2}",), inc="VC")
                E.mark_ready(f"sqr{k}", "VC")
                for tc in range(4):
                    E.add("tensor", "matmul", psA[:, tc * 512:(tc + 1) * 512],
                          ones[:, :], xbf[:, k * S + tc * 512: k * S + (tc + 1) * 512],
                          start=(k == 0), stop=(k == NP - 1),
                          rd=(f"xb{k}", "constsg") if tc == 0 else (),
                          inc="MM" if (k == NP - 1 and tc == 3) else None)
                for tc in range(4):
                    E.add("tensor", "matmul", psB[:, tc * 512:(tc + 1) * 512],
                          ones[:, :], tmpA[k % 2][:, tc * 512:(tc + 1) * 512],
                          start=(k == 0), stop=(k == NP - 1),
                          rd=(f"sqr{k}",) if tc == 0 else (),
                          inc="MM" if (tc == 3) else None)
                E.mark_free(f"tmpA{k % 2}", "MM")
            E.mark_ready("psA", "MM")
            E.mark_ready("psB", "MM")
            E.add("scalar", "mul", mean16[:, :], psA[:, :], 1.0 / D, rd=("psA", "psB"), inc="SC")
            E.add("scalar", "square", msqb[:, :], mean16[:, :], inc="SC")
            E.mark_ready("msq", "SC")
            E.add("scalar", "mul", varb[:, :], psB[:, :], 1.0 / D, inc="SC")
            E.mark_ready("varE", "SC")
            E.add("vector", "tensor_sub", varb[:, :], varb[:, :], msqb[:, :],
                  rd=("varE",), inc="VC")
            E.mark_ready("var", "VC")
            E.add("scalar", "activation", msqb[:, :], varb[:, :], AF.Ln,
                  bias=epsb[:, :], rd=("var",))
            E.add("scalar", "activation", rstd16[:, :], msqb[:, :], AF.Exp,
                  scale=-0.5, inc="SC")
            E.mark_ready("st16", "SC")
            for k in range(NP):
                E.add("vector", "tensor_sub", tmpB[:, :], xbf[:, k * S:(k + 1) * S],
                      mean16[:, :], rd=("st16",) if k == 0 else ())
                E.add("vector", "tensor_mul", xn[:, k * PS + 8: k * PS + 8 + S],
                      tmpB[:, :], rstd16[:, :], inc="VC")
            E.mark_ready("xn", "VC")
            snap0 = E.barrier_counts()
            E.emit()

        if phases >= 2:
          with contextlib.ExitStack() as st2:
            o_pre = SB(st2, "o_pre", [128, NP * S], BF16)
            wX = SB(st2, "wX", [128, 8192], BF16)
            wY = SB(st2, "wY", [128, 8192], BF16)
            E.barrier(snap0)

            with contextlib.ExitStack() as st3:
                q_sb = SB(st3, "q_sb", [128, NP * S], BF16)
                k_pad = SB(st3, "k_pad", [128, NP * PS], BF16)
                v_sh = SB(st3, "v_sh", [128, 19 * 1024], BF16)
                Pm = (SB(st3, "Pm0", [128, 224], BF16), SB(st3, "Pm1", [128, 224], BF16),
                      SB(st3, "Pm2", [128, 224], BF16), SB(st3, "Pm3", [128, 224], BF16))
                rb = (SB(st3, "rb0", [128, 224], F32), SB(st3, "rb1", [128, 224], F32),
                      SB(st3, "rb2", [128, 224], F32))

                with contextlib.ExitStack() as st4:
                    psq = (PT_(st4, "psq0", [128, 512], F32), PT_(st4, "psq1", [128, 512], F32),
                           PT_(st4, "psq2", [128, 512], F32), PT_(st4, "psq3", [128, 512], F32))
                    psv = (PT_(st4, "psv0", [128, 1024], F32), PT_(st4, "psv1", [128, 1024], F32))
                    for jw in range(NP):
                        E.add("gpsimd", "dma_start",
                              wX[:, jw * 1024:(jw + 1) * 1024],
                              wq_pk[:, jw * 1024:(jw + 1) * 1024], inc="DG")
                        E.mark_ready(f"wq{jw}", "DG")
                    E.mark_ready("wq", "DG")
                    E.add("gpsimd", "dma_start", wY[:, :], wk_pk, inc="DG")
                    E.mark_ready("wk", "DG")
                    for k in range(NP):
                        E.add("vector", "memset", k_pad[:, k * PS: k * PS + 8], 0.0,
                              inc="VC" if k == NP - 1 else None)
                        E.add("vector", "memset", k_pad[:, k * PS + 8 + S: (k + 1) * PS], 0.0,
                              inc="VC" if k == NP - 1 else None)
                    E.mark_ready("kpad0", "VC")

                    def proj(w_sb, wkey, outfn, scale, perj=False):
                        for j in range(NP):
                            for tc in range(4):
                                idx = j * 4 + tc
                                for k in range(NP):
                                    rdk = ()
                                    if k == 0 and tc == 0:
                                        rdk = (f"{wkey}{j}",) if perj else ((wkey,) if j == 0 else ())
                                    E.add("tensor", "matmul",
                                          psq[idx % 4][:, :],
                                          w_sb[:, (j * NP + k) * 128:(j * NP + k + 1) * 128],
                                          xn[:, k * PS + 8 + tc * 512: k * PS + 8 + (tc + 1) * 512],
                                          start=(k == 0), stop=(k == NP - 1),
                                          rd=rdk,
                                          fr=(f"psq{idx % 4}",) if k == 0 else (),
                                          inc="MM" if k == NP - 1 else None)
                                E.mark_ready(f"pj{idx % 4}", "MM")
                                dst, rdx = outfn(j, tc)
                                if scale is None:
                                    E.add("scalar", "copy", dst, psq[idx % 4][:, :],
                                          rd=(f"pj{idx % 4}",) + rdx, inc="SC")
                                else:
                                    E.add("scalar", "activation", dst, psq[idx % 4][:, :],
                                          AF.Copy, scale=scale,
                                          rd=(f"pj{idx % 4}",) + rdx, inc="SC")
                                E.mark_free(f"psq{idx % 4}", "SC")

                    proj(wX, "wq",
                         lambda j, tc: (q_sb[:, j * S + tc * 512: j * S + (tc + 1) * 512], ()),
                         0.125, perj=True)
                    E.mark_ready("q", "SC")
                    E.mark_free("wX", "MM")
                    E.add("gpsimd", "dma_start", wX[:, :], wvt_pk, fr=("wX",), inc="DG")
                    E.mark_ready("wvt", "DG")
                    proj(wY, "wk",
                         lambda j, tc: (k_pad[:, j * PS + 8 + tc * 512: j * PS + 8 + (tc + 1) * 512],
                                        ("kpad0",) if (j == 0 and tc == 0) else ()),
                         None)
                    E.mark_ready("k", "SC")
                    E.mark_free("wY", "MM")
                    E.add("gpsimd", "dma_start", wY[:, :], wo_pk, fr=("wY",), inc="DG")
                    E.mark_ready("wo", "DG")

                    for c in range(19):
                        M = 128 if c < 18 else 48
                        for k in range(NP):
                            for oh in range(2):
                                E.add("tensor", "matmul",
                                      psv[c % 2][0:M, oh * 512:(oh + 1) * 512],
                                      xn[:, k * PS + c * 112: k * PS + c * 112 + M],
                                      wX[:, k * 1024 + oh * 512: k * 1024 + (oh + 1) * 512],
                                      start=(k == 0), stop=(k == NP - 1),
                                      rd=("wvt",) if (c == 0 and k == 0 and oh == 0) else (),
                                      fr=(f"psv{c % 2}",) if (k == 0 and oh == 0) else (),
                                      inc="MM" if (k == NP - 1 and oh == 1) else None)
                        E.mark_ready(f"pv{c % 2}", "MM")
                        E.add("scalar", "copy", v_sh[0:M, c * 1024:(c + 1) * 1024],
                              psv[c % 2][0:M, :], rd=(f"pv{c % 2}",), inc="SC")
                        E.mark_free(f"psv{c % 2}", "SC")
                    E.mark_ready("v", "SC")
                    E.emit()

                if phases >= 3:
                  with contextlib.ExitStack() as st5:
                      psm = (PT_(st5, "psm0", [128, 1024], F32), PT_(st5, "psm1", [128, 1024], F32))
                      psO = (PT_(st5, "psO0", [128, 1024], F32), PT_(st5, "psO1", [128, 1024], F32))
                      NC2 = 19
                      NIT = NP * NC2

                      def it_info(i):
                          jp, c = divmod(i, NC2)
                          tc = 112 if c < 18 else 32
                          kk = 128 if c < 18 else 48
                          moff = 224 if c == 0 else (448 if c == 18 else 0)
                          return jp, c, i % 2, tc, kk, moff

                      def T0(i):
                          jp, c, sl, tc, kk, moff = it_info(i)
                          for hh in range(2):
                              E.add("tensor", "matmul",
                                    psm[sl][0:kk, hh * 512: hh * 512 + tc],
                                    k_pad[hh * 64:(hh + 1) * 64, jp * PS + c * 112: jp * PS + c * 112 + kk],
                                    q_sb[hh * 64:(hh + 1) * 64, jp * S + c * 112: jp * S + c * 112 + tc],
                                    start=True, stop=True,
                                    rd=("q", "k", "v") if i == 0 and hh == 0 else (),
                                    fr=(f"psm{sl}",) if hh == 0 else (),
                                    inc="MM" if hh == 1 else None)
                          E.mark_ready(f"sS{i}", "MM")

                      def SC1(i):
                          jp, c, sl, tc, kk, moff = it_info(i)
                          p3 = i % 4
                          E.add("scalar", "activation",
                                bass.AP(Pm[p3], 0, [[224, kk], [112, 2], [1, tc]]),
                                bass.AP(psm[sl], 0, [[1024, kk], [512, 2], [1, tc]]),
                                AF.Exp, bias=neg16[0:kk, :],
                                rd=(f"sS{i}",), fr=(f"Pmf{p3}",), inc="SC")
                          E.mark_ready(f"ex{i}", "SC")
                          E.mark_free(f"psm{sl}", "SC")

                      def V1(i):
                          jp, c, sl, tc, kk, moff = it_info(i)
                          p3 = i % 4
                          E.add("vector", "tensor_mul", Pm[p3][0:kk, 0:224],
                                Pm[p3][0:kk, 0:224], mk[0:kk, moff:moff + 224],
                                rd=(f"ex{i}",), inc="VC")
                          E.mark_ready(f"pm{i}", "VC")

                      def T1(i):
                          jp, c, sl, tc, kk, moff = it_info(i)
                          p3 = i % 4
                          for hh in range(2):
                              E.add("tensor", "matmul",
                                    psO[sl][:, hh * 512: hh * 512 + tc],
                                    ones[0:kk, :],
                                    Pm[p3][0:kk, hh * 112: hh * 112 + tc],
                                    start=True, stop=True,
                                    rd=(f"pm{i}",) if hh == 0 else (),
                                    fr=(f"psO{sl}",) if hh == 0 else (),
                                    inc="MM" if hh == 1 else None)
                          E.mark_ready(f"sm{i}", "MM")

                      def SC2(i):
                          jp, c, sl, tc, kk, moff = it_info(i)
                          E.add("scalar", "activation",
                                bass.AP(rb[i % 3], 0, [[224, 128], [112, 2], [1, tc]]),
                                bass.AP(psO[sl], 0, [[1024, 128], [512, 2], [1, tc]]),
                                AF.Ln, rd=(f"sm{i}",), fr=(f"rbf{i % 3}",))
                          E.add("scalar", "activation",
                                bass.AP(rb[i % 3], 0, [[224, 128], [112, 2], [1, tc]]),
                                bass.AP(rb[i % 3], 0, [[224, 128], [112, 2], [1, tc]]),
                                AF.Exp, scale=-1.0, inc="SC")
                          E.mark_ready(f"rb{i}", "SC")

                      def T2(i):
                          jp, c, sl, tc, kk, moff = it_info(i)
                          p3 = i % 4
                          for hh in range(2):
                              tp = (0, 64 * hh)
                              E.add("tensor", "matmul",
                                    psO[sl][hh * 64:(hh + 1) * 64, hh * 512: hh * 512 + tc],
                                    v_sh[0:kk, c * 1024 + jp * 128 + hh * 64: c * 1024 + jp * 128 + (hh + 1) * 64],
                                    Pm[p3][0:kk, hh * 112: hh * 112 + tc],
                                    start=True, stop=True, tile_position=tp,
                                    rd=(f"rb{i}",) if hh == 0 else (),
                                    inc="MM" if hh == 1 else None)
                          E.mark_ready(f"av{i}", "MM")
                          E.mark_free(f"Pmf{p3}", "MM")

                      def V2(i):
                          jp, c, sl, tc, kk, moff = it_info(i)
                          for hh in range(2):
                              r0 = hh * 64
                              E.add("vector", "tensor_mul",
                                    o_pre[r0:r0 + 64, jp * S + c * 112: jp * S + c * 112 + tc],
                                    psO[sl][r0:r0 + 64, hh * 512: hh * 512 + tc],
                                    rb[i % 3][r0:r0 + 64, hh * 112: hh * 112 + tc],
                                    rd=(f"av{i}",) if hh == 0 else (),
                                    inc="VC" if hh == 1 else None)
                          E.mark_free(f"psO{sl}", "VC")
                          E.mark_free(f"rbf{i % 3}", "VC")

                      for k in range(NIT + 4):
                          if k < NIT:
                              T0(k)
                          if 0 <= k - 2 < NIT:
                              T1(k - 2)
                          if 0 <= k - 3 < NIT:
                              T2(k - 3)
                          if 0 <= k - 1 < NIT:
                              SC1(k - 1)
                          if 0 <= k - 2 < NIT:
                              SC2(k - 2)
                          if 0 <= k - 1 < NIT:
                              V1(k - 1)
                          if 0 <= k - 3 < NIT:
                              V2(k - 3)
                      E.mark_ready("o_pre", "VC")
                      if debug:
                          E.add("gpsimd", "dma_start", dbg["d_xn"], xn[:, :],
                                rd=("o_pre",), inc="DG")
                          E.add("gpsimd", "dma_start", dbg["d_q"], q_sb[:, :], inc="DG")
                          E.add("gpsimd", "dma_start", dbg["d_k"], k_pad[:, :], inc="DG")
                          E.add("gpsimd", "dma_start", dbg["d_v"], v_sh[:, :], inc="DG")
                          E.add("gpsimd", "dma_start", dbg["d_op"], o_pre[:, :], inc="DG")
                      snap2 = E.barrier_counts()
                      E.emit()

            if phases >= 4:
              with contextlib.ExitStack() as st6:
                  rst = (SB(st6, "rst0", [128, S], F32), SB(st6, "rst1", [128, S], F32),
                         SB(st6, "rst2", [128, S], F32))
                  xrs = (SB(st6, "xrs0", [128, S], F32), SB(st6, "xrs1", [128, S], F32),
                         SB(st6, "xrs2", [128, S], F32))
                  pswo = (PT_(st6, "pswo0", [128, 512], F32), PT_(st6, "pswo1", [128, 512], F32),
                          PT_(st6, "pswo2", [128, 512], F32), PT_(st6, "pswo3", [128, 512], F32))
                  E.barrier(snap2)
                  E.add("sync", "dma_start", xrs[0][:, :], x_pk[:, 0:S], inc="DS")
                  E.mark_ready("xj0", "DS")
                  for j in range(NP):
                      for tc in range(4):
                          idx = j * 4 + tc
                          for k in range(NP):
                              E.add("tensor", "matmul", pswo[idx % 4][:, :],
                                    wY[:, (j * NP + k) * 128:(j * NP + k + 1) * 128],
                                    o_pre[:, k * S + tc * 512: k * S + (tc + 1) * 512],
                                    start=(k == 0), stop=(k == NP - 1),
                                    rd=("wo", "o_pre") if idx == 0 and k == 0 else (),
                                    fr=(f"pswo{idx % 4}",) if k == 0 else (),
                                    inc="MM" if k == NP - 1 else None)
                          E.mark_ready(f"po{idx % 4}", "MM")
                          E.add("scalar", "copy",
                                rst[j % 3][:, tc * 512:(tc + 1) * 512],
                                pswo[idx % 4][:, :],
                                rd=(f"po{idx % 4}",), fr=(f"rstf{j % 3}",) if tc == 0 else (),
                                inc="SC")
                          E.mark_ready(f"rc{idx}", "SC")
                          E.mark_free(f"pswo{idx % 4}", "SC")
                          E.add("vector", "tensor_add",
                                rst[j % 3][:, tc * 512:(tc + 1) * 512],
                                rst[j % 3][:, tc * 512:(tc + 1) * 512],
                                xrs[j % 3][:, tc * 512:(tc + 1) * 512],
                                rd=(f"rc{idx}", f"xj{j}"), inc="VC")
                      E.mark_ready(f"rj{j}", "VC")
                      E.mark_free(f"xjf{j % 3}", "VC")
                      if j + 1 < NP:
                          E.add("sync", "dma_start", xrs[(j + 1) % 3][:, :],
                                x_pk[:, (j + 1) * S:(j + 2) * S],
                                fr=(f"xjf{(j + 1) % 3}",), inc="DS")
                          E.mark_ready(f"xj{j + 1}", "DS")
                      E.add("sync", "dma_start", r_scr[:, j * S:(j + 1) * S],
                            rst[j % 3][:, :], rd=(f"rj{j}",), inc="DS")
                      E.mark_free(f"rstf{j % 3}", "DS")
                  if debug:
                      E.add("gpsimd", "dma_start", dbg["d_r"], r_scr,
                            waits=[("DS", E.cnt["DS"])], inc="DG")
                  snap3 = E.barrier_counts()
                  E.emit()

        if phases >= 5:
          with contextlib.ExitStack() as st7:
            a_sb = SB(st7, "a_sb", [128, 32 * HALF], BF16)
            b_sb = SB(st7, "b_sb", [128, 32 * HALF], BF16)
            w1r = (SB(st7, "w1r0", [128, 1024], BF16), SB(st7, "w1r1", [128, 1024], BF16))
            w2r = (SB(st7, "w2r0", [128, 1024], BF16), SB(st7, "w2r1", [128, 1024], BF16))
            hr = (SB(st7, "hr0", [128, 4096], BF16), SB(st7, "hr1", [128, 4096], BF16))
            ta = (SB(st7, "ta0", [128, 512], F32), SB(st7, "ta1", [128, 512], F32))
            tb = (SB(st7, "tb0", [128, 512], F32), SB(st7, "tb1", [128, 512], F32))
            xre = (SB(st7, "xre0", [128, 512], F32), SB(st7, "xre1", [128, 512], F32))
            E.barrier(snap3)
            for H in range(2):
                toff = H * HALF
                with contextlib.ExitStack() as st8:
                    pg1 = (PT_(st8, f"pg10_{H}", [128, 1024], F32),
                           PT_(st8, f"pg11_{H}", [128, 1024], F32))
                    pg2 = (PT_(st8, f"pg20_{H}", [128, 1024], F32),
                           PT_(st8, f"pg21_{H}", [128, 1024], F32))
                    sc_stage = E.cnt["SC"]
                    for j in range(NJH):
                        E.add("gpsimd", "dma_start", w1r[j % 2][:, :],
                              w1_pk[:, j * 1024:(j + 1) * 1024],
                              fr=(f"w1r{j % 2}",), inc="DG")
                        E.add("gpsimd", "dma_start", w2r[j % 2][:, :],
                              w2_pk[:, j * 1024:(j + 1) * 1024],
                              fr=(f"w2r{j % 2}",), inc="DG")
                        E.mark_ready(f"w12_{j % 2}", "DG")
                        for k in range(NP):
                            for tc in range(2):
                                E.add("tensor", "matmul",
                                      pg1[j % 2][:, tc * 512:(tc + 1) * 512],
                                      w1r[j % 2][:, k * 128:(k + 1) * 128],
                                      xn[:, k * PS + 8 + toff + tc * 512: k * PS + 8 + toff + (tc + 1) * 512],
                                      start=(k == 0), stop=(k == NP - 1),
                                      waits=[("SC", sc_stage)] if (j == 0 and k == 0 and tc == 0) else (),
                                      rd=(f"w12_{j % 2}",) if k == 0 and tc == 0 else (),
                                      fr=(f"pg1{j % 2}", f"pg2{j % 2}") if k == 0 and tc == 0 else ())
                        for k in range(NP):
                            for tc in range(2):
                                E.add("tensor", "matmul",
                                      pg2[j % 2][:, tc * 512:(tc + 1) * 512],
                                      w2r[j % 2][:, k * 128:(k + 1) * 128],
                                      xn[:, k * PS + 8 + toff + tc * 512: k * PS + 8 + toff + (tc + 1) * 512],
                                      start=(k == 0), stop=(k == NP - 1),
                                      inc="MM" if (k == NP - 1 and tc == 1) else None)
                        E.mark_ready(f"pga{j % 2}", "MM")
                        E.mark_free(f"w1r{j % 2}", "MM")
                        E.mark_free(f"w2r{j % 2}", "MM")
                        for tc in range(2):
                            i2 = (2 * j + tc) % 2
                            E.add("scalar", "activation", ta[i2][:, :],
                                  pg1[j % 2][:, tc * 512:(tc + 1) * 512], AF.Silu,
                                  scale=s1s[:, j:j + 1],
                                  rd=(f"pga{j % 2}",), fr=(f"taf{i2}",))
                            E.add("scalar", "activation", tb[i2][:, :],
                                  pg2[j % 2][:, tc * 512:(tc + 1) * 512], AF.Copy,
                                  scale=s2s[:, j:j + 1], inc="SC")
                            E.mark_ready(f"tl{i2}", "SC")
                            if tc == 1:
                                E.mark_free(f"pg1{j % 2}", "SC")
                                E.mark_free(f"pg2{j % 2}", "SC")
                            E.add("vector", "tensor_mul",
                                  a_sb[:, j * 1024 + tc * 512: j * 1024 + (tc + 1) * 512],
                                  ta[i2][:, :], tb[i2][:, :], rd=(f"tl{i2}",), inc="VC")
                            E.mark_free(f"taf{i2}", "VC")
                    E.mark_ready("a", "VC")
                    E.emit()

                for stage, (hpk, ssb, src, dst) in enumerate((
                        (h1_pk, sh1s, a_sb, b_sb),
                        (h2_pk, sh2s, b_sb, a_sb))):
                    with contextlib.ExitStack() as st9:
                        psb = (PT_(st9, f"psb0_{H}_{stage}", [128, 1024], F32),
                               PT_(st9, f"psb1_{H}_{stage}", [128, 1024], F32))
                        for j in range(NJH):
                            E.add("gpsimd", "dma_start", hr[j % 2][:, :],
                                  hpk[:, j * 4096:(j + 1) * 4096],
                                  fr=(f"hr{j % 2}",), inc="DG")
                            E.mark_ready(f"hl{j % 2}", "DG")
                            for k in range(NJH):
                                for tc in range(2):
                                    E.add("tensor", "matmul",
                                          psb[j % 2][:, tc * 512:(tc + 1) * 512],
                                          hr[j % 2][:, k * 128:(k + 1) * 128],
                                          src[:, k * 1024 + tc * 512: k * 1024 + (tc + 1) * 512],
                                          start=(k == 0), stop=(k == NJH - 1),
                                          rd=((f"hl{j % 2}",) + (("a",) if stage == 0 else ("c_in",))
                                              ) if k == 0 and tc == 0 else (),
                                          fr=(f"psb{j % 2}",) if k == 0 and tc == 0 else (),
                                          inc="MM" if (k == NJH - 1 and tc == 1) else None)
                            E.mark_ready(f"pb{j % 2}", "MM")
                            E.mark_free(f"hr{j % 2}", "MM")
                            for tc in range(2):
                                E.add("scalar", "activation",
                                      dst[:, j * 1024 + tc * 512: j * 1024 + (tc + 1) * 512],
                                      psb[j % 2][:, tc * 512:(tc + 1) * 512], AF.Gelu,
                                      scale=ssb[:, j:j + 1], rd=(f"pb{j % 2}",),
                                      inc="SC" if tc == 1 else None)
                            E.mark_free(f"psb{j % 2}", "SC")
                        E.mark_ready("c_in" if stage == 0 else "cc", "SC")
                        E.emit()

                with contextlib.ExitStack() as st10:
                    psf = (PT_(st10, f"psf0_{H}", [128, 1024], F32),
                           PT_(st10, f"psf1_{H}", [128, 1024], F32))
                    for j in range(NP):
                        E.add("gpsimd", "dma_start", hr[j % 2][:, :],
                              w3_pk[:, j * 4096:(j + 1) * 4096],
                              fr=(f"hr{j % 2}",), inc="DG")
                        E.mark_ready(f"w3l{j % 2}", "DG")
                        for k in range(NJH):
                            for tc in range(2):
                                E.add("tensor", "matmul",
                                      psf[j % 2][:, tc * 512:(tc + 1) * 512],
                                      hr[j % 2][:, k * 128:(k + 1) * 128],
                                      a_sb[:, k * 1024 + tc * 512: k * 1024 + (tc + 1) * 512],
                                      start=(k == 0), stop=(k == NJH - 1),
                                      rd=(f"w3l{j % 2}", "cc") if k == 0 and tc == 0 else (),
                                      fr=(f"psf{j % 2}",) if k == 0 and tc == 0 else (),
                                      inc="MM" if (k == NJH - 1 and tc == 1) else None)
                        E.mark_ready(f"pf{j % 2}", "MM")
                        E.mark_free(f"hr{j % 2}", "MM")
                        for tc in range(2):
                            idx = j * 2 + tc
                            i2 = idx % 2
                            coff = j * S + toff + tc * 512
                            E.add("sync", "dma_start", xre[i2][:, :],
                                  r_scr[:, coff: coff + 512], fr=(f"xref{i2}",), inc="DS")
                            E.mark_ready(f"xr{idx}", "DS")
                            E.add("scalar", "activation", ta[i2][:, :],
                                  psf[j % 2][:, tc * 512:(tc + 1) * 512], AF.Copy,
                                  scale=s3s[:, j:j + 1],
                                  rd=(f"pf{j % 2}",), fr=(f"taf{i2}",),
                                  inc="SC")
                            E.mark_ready(f"ft{idx}", "SC")
                            if tc == 1:
                                E.mark_free(f"psf{j % 2}", "SC")
                            E.add("vector", "tensor_add", tb[i2][:, :],
                                  ta[i2][:, :], xre[i2][:, :],
                                  rd=(f"ft{idx}", f"xr{idx}"), fr=(f"tbf{i2}",), inc="VC")
                            E.mark_ready(f"ot{idx}", "VC")
                            E.mark_free(f"taf{i2}", "VC")
                            E.mark_free(f"xref{i2}", "VC")
                            E.add("sync", "dma_start",
                                  out_pk[:, coff: coff + 512], tb[i2][:, :],
                                  rd=(f"ot{idx}",), inc="DS")
                            E.mark_free(f"tbf{i2}", "DS")
                    if H == 1 and debug:
                        E.add("gpsimd", "dma_start", dbg["d_a"], a_sb[:, :], inc="DG")
                        E.add("gpsimd", "dma_start", dbg["d_b"], b_sb[:, :], inc="DG")
                    E.emit()
            E.add("gpsimd", "nop", waits=[("DG", E.cnt["DG"]), ("DS", E.cnt["DS"])])
            E.add("sync", "nop", waits=[("DG", E.cnt["DG"]), ("DS", E.cnt["DS"])])
            E.emit()

    E.emit_all()
    for s in sem_cm:
        s.__exit__(None, None, None)
    return nc


def _pack_lhsT(W):
    O, I = W.shape
    A = W.reshape(O // 128, 128, I // 128, 128)
    return np.ascontiguousarray(A.transpose(3, 0, 2, 1).reshape(128, -1))


def _pack_fm(X):
    T, C = X.shape
    return np.ascontiguousarray(X.T.reshape(C // 128, 128, T).transpose(1, 0, 2).reshape(128, -1))


def _masks():
    j = np.arange(128)[:, None]
    tp_ = np.arange(112)[None, :]
    band = (j >= tp_) & (j <= tp_ + 15)
    mid = band
    first = band & (j >= 8)
    last = band & (tp_ < 32) & (j < 40)
    pair = lambda m: np.concatenate([m, m], axis=1).astype(np.float32)
    return np.ascontiguousarray(np.concatenate([pair(mid), pair(first), pair(last)], axis=1))


def kernel(L0_attn_x, L0_attn_Wq, L0_attn_Wk, L0_attn_Wv, L0_attn_Wo,
           L0_ffn_w1_w, L0_ffn_w1_s, L0_ffn_w2_w, L0_ffn_w2_s,
           L0_ffn_h1_w, L0_ffn_h1_s, L0_ffn_h2_w, L0_ffn_h2_s,
           L0_ffn_w3_w, L0_ffn_w3_s):
    f32 = lambda a: np.asarray(a, dtype=np.float32)
    x = f32(L0_attn_x)
    shared = {
        "wq_pk": _pack_lhsT(f32(L0_attn_Wq)),
        "wk_pk": _pack_lhsT(f32(L0_attn_Wk)),
        "wvt_pk": np.ascontiguousarray(
            f32(L0_attn_Wv).T.reshape(8, 128, 1024).transpose(1, 0, 2).reshape(128, 8192)),
        "wo_pk": _pack_lhsT(f32(L0_attn_Wo)),
        "w1_pk": _pack_lhsT(f32(L0_ffn_w1_w)),
        "w2_pk": _pack_lhsT(f32(L0_ffn_w2_w)),
        "h1_pk": _pack_lhsT(f32(L0_ffn_h1_w)),
        "h2_pk": _pack_lhsT(f32(L0_ffn_h2_w)),
        "w3_pk": _pack_lhsT(f32(L0_ffn_w3_w)),
        "s1_pk": np.ascontiguousarray(f32(L0_ffn_w1_s).reshape(32, 128).T),
        "s2_pk": np.ascontiguousarray(f32(L0_ffn_w2_s).reshape(32, 128).T),
        "sh1_pk": np.ascontiguousarray(f32(L0_ffn_h1_s).reshape(32, 128).T),
        "sh2_pk": np.ascontiguousarray(f32(L0_ffn_h2_s).reshape(32, 128).T),
        "s3_pk": np.ascontiguousarray(f32(L0_ffn_w3_s).reshape(8, 128).T),
        "maskp": _masks(),
        "ones_in": np.ones((128, 128), dtype=np.float32),
    }
    nc = build_nc()
    in_maps = [dict(shared, x_pk=_pack_fm(x[b])) for b in range(8)]
    res = run_bass_kernel_spmd(nc, in_maps, core_ids=list(range(8)))
    outs = []
    for r in res.results:
        o = np.asarray(r["out_pk"])
        outs.append(o.reshape(128, NP, S).transpose(2, 1, 0).reshape(S, D))
    return np.ascontiguousarray(np.stack(outs)).astype(np.float32)
